# version 04 (final)
# baseline (speedup 1.0000x reference)
import numpy as np
import ml_dtypes
import concourse.bass as bass
import concourse.mybir as mybir
from concourse.bass_utils import run_bass_kernel_spmd
from contextlib import ExitStack

F32 = mybir.dt.float32
BF16 = mybir.dt.bfloat16
I32 = mybir.dt.int32
AF = mybir.ActivationFunctionType
ALU = mybir.AluOpType
BF16NP = ml_dtypes.bfloat16

B = 2
S = 4096
H = 2048
NH = 16
HD = 128
INTER = 5504
NC = 8
HC = H // 128
OC_I = INTER // 128
OWN_T = 1024
SCALE = 1.0 / np.sqrt(128.0)
EXP_BIAS = -20.0
NEG = -1.0e30
EPS = 1e-5

DEBUG = False
DBG_SET = {"qkv", "agdump", "pre", "stream"}
STAGE = None
EARLY_AG = False
STATIC_TQ0 = False
CC_DELAY = 0
AO_LOAD_ENG = "sync"
N_PAIRS = 4

_CACHE = {}
_LAST_SCHED = None


class Ev:
    __slots__ = ("eng", "fn", "deps", "dma", "val", "need", "tag", "asyn")

    def __init__(self, eng, fn, deps, dma, tag, asyn=False):
        self.eng = eng
        self.fn = fn
        self.deps = [d for d in deps if d is not None]
        self.dma = dma
        self.val = None
        self.need = False
        self.tag = tag
        self.asyn = asyn


ENGINES = ("sync", "tensor", "vector", "scalar", "gpsimd")


class Sched:
    def __init__(self):
        self.evs = {e: [] for e in ENGINES}
        self.marks = {}

    def add(self, eng, fn, deps=(), dma=False, tag="", asyn=False):
        ev = Ev(eng, fn, deps, dma, tag, asyn)
        self.evs[eng].append(ev)
        return ev

    def mark(self, name):
        self.marks[name] = {e: len(self.evs[e]) for e in ENGINES}

    def truncate(self, name):
        m = self.marks[name]
        for e in ENGINES:
            self.evs[e] = self.evs[e][:m[e]]

    NLANES = 8

    def emit(self, nc, block, sems, preludes):
        if STAGE is not None and STAGE in self.marks:
            self.truncate(STAGE)
        for e in ENGINES:
            for ev in self.evs[e]:
                for d in ev.deps:
                    if d.eng == ev.eng and not d.dma and not d.asyn:
                        continue
                    d.need = True
                if ev.asyn:
                    ev.need = True
        K = self.NLANES
        n_asyn = 0
        for e in ENGINES:
            c = 0
            dm = 0
            for ev in self.evs[e]:
                if ev.dma:
                    lane = dm % K
                    use = dm // K
                    ev.val = (lane, 16 * (use + 1))
                    dm += 1
                elif ev.asyn:
                    ev.val = ("cc", n_asyn)
                    n_asyn += 1
                elif ev.need:
                    c += 1
                    ev.val = c
        for e in ENGINES:
            evs = self.evs[e]
            if not evs:
                continue

            def body(eng, evs=evs, e=e):
                if e in preludes:
                    preludes[e](eng)
                hi = {}
                for ev in evs:
                    for d in ev.deps:
                        if d.eng == e and not d.dma and not d.asyn:
                            continue
                        if d.dma:
                            key = (d.eng, "lane", d.val[0])
                            tgt = d.val[1]
                        elif d.asyn:
                            key = ("cc", d.val[1])
                            tgt = 1
                        else:
                            key = (d.eng, "c")
                            tgt = d.val
                        if hi.get(key, 0) >= tgt:
                            continue
                        eng.wait_ge(sems[key], tgt)
                        hi[key] = tgt
                    if ev.dma:
                        lane, tgt = ev.val
                        key = (e, "lane", lane)
                        if tgt > 16 and hi.get(key, 0) < tgt - 16:
                            eng.wait_ge(sems[key], tgt - 16)
                            hi[key] = tgt - 16
                    inst = ev.fn(eng)
                    if ev.dma:
                        inst.then_inc(sems[(e, "lane", ev.val[0])], 16)
                    elif ev.asyn:
                        inst.then_inc(sems[("cc", ev.val[1])], 1)
                    elif ev.need:
                        inst.then_inc(sems[(e, "c")], 1)

            getattr(block, e)(body)


def MM(ps, lh, rh, st, sp):
    return lambda eng, ps=ps, lh=lh, rh=rh, st=st, sp=sp: eng.matmul(
        out=ps, lhsT=lh, rhs=rh, start=st, stop=sp)


def TRANS(ps, in_, ident):
    return lambda eng, ps=ps, in_=in_, ident=ident: eng.transpose(
        out=ps, in_=in_, identity=ident)


def ACTF(out, in_, func=AF.Copy, **kw):
    return lambda eng, out=out, in_=in_, func=func, kw=kw: eng.activation(
        out=out, in_=in_, func=func, **kw)


def DMA(out, in_):
    return lambda eng, out=out, in_=in_: eng.dma_start(out=out, in_=in_)


def TT(out, a, b, op):
    return lambda eng, out=out, a=a, b=b, op=op: eng.tensor_tensor(
        out=out, in0=a, in1=b, op=op)


def TS(out, in_, s1, s2, op0, op1):
    return lambda eng, out=out, in_=in_, s1=s1, s2=s2, op0=op0, op1=op1: \
        eng.tensor_scalar(out=out, in0=in_, scalar1=s1, scalar2=s2,
                          op0=op0, op1=op1)


def RECIP(out, in_):
    return lambda eng, out=out, in_=in_: eng.reciprocal(out=out, in_=in_)


def TCOPY(out, in_):
    return lambda eng, out=out, in_=in_: eng.tensor_copy(out=out, in_=in_)


def build_program():
    nc = bass.Bass("TRN2", target_bir_lowering=False, debug=False,
                   enable_partition_id=True)
    sch = Sched()

    d_xT = nc.dram_tensor("xT", [H, S], F32, kind="ExternalInput").ap()
    d_xTres = nc.dram_tensor("xTres", [H, OWN_T], F32, kind="ExternalInput").ap()
    d_wqT = nc.dram_tensor("wqT", [H, 512], F32, kind="ExternalInput").ap()
    d_wkT = nc.dram_tensor("wkT", [H, 512], F32, kind="ExternalInput").ap()
    d_wvT = nc.dram_tensor("wvT", [H, 512], F32, kind="ExternalInput").ap()
    d_woT = nc.dram_tensor("woT", [H, H], F32, kind="ExternalInput").ap()
    d_gidxT = nc.dram_tensor("gidxT", [256, INTER], I32, kind="ExternalInput").ap()
    d_uidxT = nc.dram_tensor("uidxT", [256, INTER], I32, kind="ExternalInput").ap()
    d_didxT = nc.dram_tensor("didxT", [INTER, 256], I32, kind="ExternalInput").ap()
    d_lut = nc.dram_tensor("lut", [1, 16], F32, kind="ExternalInput").ap()
    d_y = nc.dram_tensor("y", [OWN_T, H], F32, kind="ExternalOutput").ap()

    cc_g_in = nc.dram_tensor("cc_g_in", [2, 128, INTER], BF16, kind="Internal").ap()
    cc_u_in = nc.dram_tensor("cc_u_in", [2, 128, INTER], BF16, kind="Internal").ap()
    cc_d_in = nc.dram_tensor("cc_d_in", [OC_I, 128, 256], BF16, kind="Internal").ap()
    cc_g_out = nc.dram_tensor("cc_g_out", [16, 128, INTER], BF16, kind="Internal",
                              addr_space="Shared").ap()
    cc_u_out = nc.dram_tensor("cc_u_out", [16, 128, INTER], BF16, kind="Internal",
                              addr_space="Shared").ap()
    cc_d_out = nc.dram_tensor("cc_d_out", [8, OC_I, 128, 256], BF16, kind="Internal",
                              addr_space="Shared").ap()
    cc_a_in = nc.dram_tensor("cc_a_in", [4, 128, S], BF16, kind="Internal").ap()
    cc_a_outp = [nc.dram_tensor(f"cc_a_out{p}", [4, 128, S], BF16, kind="Internal").ap()
                 for p in range(4)]

    dbg = {}
    if DEBUG:
        dbg["qt"] = nc.dram_tensor("dbg_qt", [128, 4 * S], BF16, kind="ExternalOutput").ap()
        dbg["kt"] = nc.dram_tensor("dbg_kt", [128, 4 * S], BF16, kind="ExternalOutput").ap()
        dbg["v4"] = nc.dram_tensor("dbg_v4", [128, 4 * S], BF16, kind="ExternalOutput").ap()
        dbg["ao"] = nc.dram_tensor("dbg_ao", [4, 128, S], BF16, kind="ExternalOutput").ap()
        dbg["wg"] = nc.dram_tensor("dbg_wg", [2, 128, INTER], BF16, kind="ExternalOutput").ap()
        dbg["wd"] = nc.dram_tensor("dbg_wd", [OC_I, 128, 256], BF16, kind="ExternalOutput").ap()
        dbg["stream"] = nc.dram_tensor("dbg_stream", [128, HC * OWN_T], F32, kind="ExternalOutput").ap()
        dbg["pre"] = nc.dram_tensor("dbg_pre", [128, HC * OWN_T], F32, kind="ExternalOutput").ap()
        dbg["aot"] = nc.dram_tensor("dbg_aot", [128, HC * OWN_T], BF16, kind="ExternalOutput").ap()
        dbg["aout"] = nc.dram_tensor("dbg_aout", [4, 4, 128, S], BF16, kind="ExternalOutput").ap()
        dbg["xmid"] = nc.dram_tensor("dbg_xmid", [128, HC * OWN_T], BF16, kind="ExternalOutput").ap()

    ident_f32 = nc.inline_tensor(np.eye(128, dtype=np.float32), name="ident_f32").ap()
    onesrow_f32 = nc.inline_tensor(np.ones((1, 128), np.float32), name="onesrow").ap()
    onescol_f32 = nc.inline_tensor(np.ones((128, 1), np.float32), name="onescol_f").ap()
    onescol_bf = nc.inline_tensor(np.ones((128, 1), BF16NP), name="onescol_b").ap()
    kk = np.arange(128)[:, None]
    jj = np.arange(512)[None, :]
    masks_np = np.stack([np.where(jj < 128 * i + kk, NEG, 0.0) for i in range(4)])
    d_masks = nc.inline_tensor(masks_np.astype(BF16NP), name="masks").ap()

    pid_cell = {}

    def mk_prelude(name):
        def f(eng):
            pid_cell[name] = eng.partition_id()
        return f

    preludes = {"gpsimd": mk_prelude("gpsimd"), "sync": mk_prelude("sync")}

    with ExitStack() as glob:
        def sb(stack, name, shape, dt):
            return stack.enter_context(nc.sbuf_tensor(name, shape, dt))

        sems = {}
        for i in range(8):
            sems[("cc", i)] = glob.enter_context(nc.semaphore(f"s_cc{i}"))
        for e in ENGINES:
            sems[(e, "c")] = glob.enter_context(nc.semaphore(f"s_{e}_c"))
            for ln in range(Sched.NLANES):
                sems[(e, "lane", ln)] = glob.enter_context(
                    nc.semaphore(f"s_{e}_l{ln}"))

        PS = [glob.enter_context(nc.psum_tensor(f"ps{i}", [128, 512], F32))
              for i in range(8)]

        c_ident = sb(glob, "c_ident", [128, 128], F32)
        c_onesrow = sb(glob, "c_onesrow", [1, 128], F32)
        c_onescol_f = sb(glob, "c_onescol_f", [128, 1], F32)
        c_onescol_b = sb(glob, "c_onescol_b", [128, 1], BF16)
        c_masks = sb(glob, "c_masks", [128, 4 * 512], BF16)
        c_lut_row = sb(glob, "c_lut_row", [1, 16], F32)
        c_lut_bc = sb(glob, "c_lut_bc", [128, 16], F32)
        c_bias = sb(glob, "c_bias", [128, 2], F32)

        e_consts = []
        e_consts.append(sch.add("sync", DMA(c_ident[:], ident_f32[:, :]), dma=True))
        e_consts.append(sch.add("sync", DMA(c_onesrow[:], onesrow_f32[:, :]), dma=True))
        e_consts.append(sch.add("sync", DMA(c_onescol_f[:], onescol_f32[:, :]), dma=True))
        e_consts.append(sch.add("sync", DMA(c_onescol_b[:], onescol_bf[:, :]), dma=True))
        for i in range(4):
            e_consts.append(sch.add(
                "sync", DMA(c_masks[:, 512 * i:512 * (i + 1)], d_masks[i, :, :]),
                dma=True))
        bias_np = np.tile(np.array([[EXP_BIAS, EPS]], np.float32), (128, 1))
        d_bias = nc.inline_tensor(bias_np, name="biasconst").ap()
        e_biasld = sch.add("sync", DMA(c_bias[:], d_bias[:, :]), dma=True)
        e_lutld = sch.add("sync", DMA(c_lut_row[:], d_lut[:, :]), dma=True)
        e_lutmm = sch.add("tensor", MM(PS[7][:, 0:16], c_onesrow[:, :], c_lut_row[:, :],
                                       True, True), deps=[e_lutld, e_consts[1]])
        e_lutbc = sch.add("scalar", ACTF(c_lut_bc[:], PS[7][:, 0:16]), deps=[e_lutmm])

        if EARLY_AG:
            cc_e_in = nc.dram_tensor("cc_e_in", [64, 256], F32, kind="Internal").ap()
            cc_e_out = nc.dram_tensor("cc_e_out", [512, 256], F32, kind="Internal").ap()
            e_fill = sch.add("gpsimd", DMA(cc_e_in[:, :], d_lut[0:1, 0:16]), dma=True) if False else None
            e_eag = sch.add("gpsimd",
                            lambda eng: eng.collective_compute(
                                "AllGather", ALU.bypass,
                                replica_groups=[list(range(8))],
                                ins=[cc_e_in[:, :]], outs=[cc_e_out[:, :]]),
                            deps=[e_consts[0]], asyn=True)
            sch.mark("early")
        sch.mark("consts")
        s12 = ExitStack()
        QT4 = sb(s12, "QT4", [128, 4 * S], BF16)
        KT4 = sb(s12, "KT4", [128, 4 * S], BF16)
        V4 = sb(s12, "V4", [128, 4 * S], BF16)

        s1 = ExitStack()
        w_bf = {nm: sb(s1, f"w_{nm}", [128, HC * 512], BF16) for nm in ("q", "k", "v")}
        w_stage = [sb(s1, f"wst{i}", [128, 512], F32) for i in range(2)]
        xt_bf = [sb(s1, f"xtbf{i}", [128, HC * 512], BF16) for i in range(2)]
        xt_stage = [sb(s1, f"xtst{i}", [128, 512], F32) for i in range(2)]

        wsrc = {"q": d_wqT, "k": d_wkT, "v": d_wvT}
        last_cast = {}
        ld_hist = []
        for wi, nm in enumerate(("q", "k", "v")):
            for hc in range(HC):
                i = (wi * HC + hc) % 2
                dep_reuse = ld_hist[-2][1] if len(ld_hist) >= 2 else None
                e_ld = sch.add("sync", DMA(w_stage[i][:],
                                           wsrc[nm][128 * hc:128 * (hc + 1), :]),
                               deps=[dep_reuse], dma=True)
                e_cst = sch.add("vector", TCOPY(w_bf[nm][:, 512 * hc:512 * (hc + 1)],
                                                w_stage[i][:]), deps=[e_ld])
                ld_hist.append((e_ld, e_cst))
                last_cast[(nm, hc)] = e_cst

        qkv_mm_tail = []
        qkv_units = []
        xt_cast = {}
        unit_hist = []
        xt_users = {}
        for tc in range(8):
            xt_users[tc] = []
            for hc in range(HC):
                i = (tc * HC + hc) % 2
                deps = []
                if (tc * HC + hc) >= 2:
                    pj = tc * HC + hc - 2
                    deps.append(xt_cast.get((pj // HC, pj % HC)))
                if hc == 0 and tc >= 2:
                    deps.extend(xt_users[tc - 2])
                e_ld = sch.add("sync", DMA(xt_stage[i][:],
                                           d_xT[:, 512 * tc:512 * (tc + 1)]
                                           [128 * hc:128 * (hc + 1), :]),
                               deps=deps, dma=True)
                e_cst = sch.add("vector",
                                TCOPY(xt_bf[tc % 2][:, 512 * hc:512 * (hc + 1)],
                                      xt_stage[i][:]), deps=[e_ld])
                xt_cast[(tc, hc)] = e_cst

            def qkv_unit(kind, idx, tc=tc):
                bank = len(unit_hist) % 3
                ps = PS[bank]
                deps_first = []
                if len(unit_hist) >= 3:
                    deps_first.append(unit_hist[-3][1])
                mmev = None
                for hc in range(HC):
                    if kind == "v":
                        lh = xt_bf[tc % 2][:, 512 * hc + 128 * idx:512 * hc + 128 * (idx + 1)]
                        rh = w_bf["v"][:, 512 * hc:512 * (hc + 1)]
                        dep_w = last_cast[("v", hc)]
                    else:
                        lh = w_bf[kind][:, 512 * hc + 128 * idx:512 * hc + 128 * (idx + 1)]
                        rh = xt_bf[tc % 2][:, 512 * hc:512 * (hc + 1)]
                        dep_w = last_cast[(kind, hc)]
                    deps = [xt_cast[(tc, hc)], dep_w]
                    if hc == 0:
                        deps += deps_first
                    mmev = sch.add("tensor", MM(ps[:, :], lh, rh, hc == 0, hc == HC - 1),
                                   deps=deps)
                    xt_users[tc].append(mmev)
                if kind == "q":
                    dst = QT4[:, 4096 * idx + 512 * tc: 4096 * idx + 512 * (tc + 1)]
                elif kind == "k":
                    dst = KT4[:, 4096 * idx + 512 * tc: 4096 * idx + 512 * (tc + 1)]
                else:
                    kt = 4 * tc + idx
                    dst = V4[:, 512 * kt:512 * (kt + 1)]
                e_ev = sch.add("scalar", ACTF(dst, ps[:, :]), deps=[mmev])
                unit_hist.append((mmev, e_ev))
                qkv_units.append(e_ev)

            for p in range(4):
                qkv_unit("q", p)
            for p in range(4):
                qkv_unit("k", p)
            for tt in range(4):
                qkv_unit("v", tt)

        qkv_pe_tail = unit_hist[-1][0]
        qkv_sc_tail = unit_hist[-1][1]
        if DEBUG and "qkv" in DBG_SET:
            eq = sch.add("sync", DMA(dbg["qt"][:, :], QT4[:]), deps=qkv_units, dma=True)
            ek = sch.add("sync", DMA(dbg["kt"][:, :], KT4[:]), deps=qkv_units, dma=True)
            ev_ = sch.add("sync", DMA(dbg["v4"][:, :], V4[:]), deps=qkv_units, dma=True)

        s1.close()
        sch.mark("qkv")

        s2 = ExitStack()
        pt_bufs = [sb(s2, f"pt{i}", [128, 512], BF16) for i in range(4)]
        r_sb = sb(s2, "r_sb", [1, 512], F32)
        rbc_sb = sb(s2, "rbc_sb", [128, 512], F32)
        ao_sb = [sb(s2, f"ao{i}", [128, 512], BF16) for i in range(2)]
        g_idx = [sb(s2, f"gidx{i}", [128, 2752], I32) for i in range(2)]
        g_idxbf = sb(s2, "gidxbf", [128, 2752], BF16)
        g_tmp = sb(s2, "gtmp", [128, 2752], BF16)
        g_acc = [sb(s2, f"gacc{i}", [128, 2752], BF16) for i in range(2)]

        gather_ops = []
        gstate = {"prev_outs": {0: [], 1: []},
                  "prev_cast": {0: None, 1: None}}

        def build_gather():
            piece_list = []
            for nm, src, dst in (("g", d_gidxT, cc_g_in), ("u", d_uidxT, cc_u_in)):
                for slab in range(2):
                    for half in range(2):
                        piece_list.append((nm, src, dst, slab, half))
            for i in range((OC_I + 7) // 8):
                piece_list.append(("d", d_didxT, cc_d_in, 8 * i,
                                   min(8, OC_I - 8 * i)))

            for pi, piece in enumerate(piece_list):
                buf = pi % 2
                nm = piece[0]
                fd = 2752 if nm in ("g", "u") else 256 * piece[4]
                idx_t = g_idx[buf]
                acc_t = g_acc[buf]
                ctx = {}

                def st_loads(piece=piece, idx_t=idx_t, ctx=ctx, buf=buf):
                    deps = list(gstate["prev_outs"][buf]) + list(gstate.get("phase", ()))
                    if gstate["prev_cast"][buf] is not None:
                        deps.append(gstate["prev_cast"][buf])
                    evs = []
                    if piece[0] in ("g", "u"):
                        _, src, _, slab, half = piece
                        evs.append(sch.add(
                            "sync",
                            DMA(idx_t[:, 0:2752],
                                src[128 * slab:128 * (slab + 1),
                                    2752 * half:2752 * (half + 1)]),
                            deps=deps, dma=True))
                    else:
                        _, src, _, start, cnt = piece
                        for s_ in range(cnt):
                            evs.append(sch.add(
                                "sync",
                                DMA(idx_t[:, 256 * s_:256 * (s_ + 1)],
                                    src[128 * (start + s_):
                                        128 * (start + s_ + 1), :]),
                                deps=deps if s_ == 0 else (), dma=True))
                    ctx["loads"] = evs
                gather_ops.append(st_loads)

                def st_cast(ctx=ctx, fd=fd, idx_t=idx_t, buf=buf):
                    ctx["cast"] = sch.add(
                        "vector", TCOPY(g_idxbf[:, 0:fd], idx_t[:, 0:fd]),
                        deps=ctx["loads"])
                    gstate["prev_cast"][buf] = ctx["cast"]
                gather_ops.append(st_cast)

                def mk_vop(v, fd=fd, ctx=ctx, acc_t=acc_t, buf=buf):
                    def op():
                        if v == 0:
                            ctx["acc"] = sch.add(
                                "vector",
                                TS(acc_t[:, 0:fd], g_idxbf[:, 0:fd],
                                   0.0, c_lut_bc[:, 0:1], ALU.is_equal,
                                   ALU.mult),
                                deps=[ctx["cast"], e_lutbc]
                                + list(gstate["prev_outs"][buf]))
                        else:
                            e1 = sch.add(
                                "vector",
                                TS(g_tmp[:, 0:fd], g_idxbf[:, 0:fd],
                                   float(v), c_lut_bc[:, v:v + 1],
                                   ALU.is_equal, ALU.mult),
                                deps=[ctx["cast"]])
                            ctx["acc"] = sch.add(
                                "vector",
                                TT(acc_t[:, 0:fd], acc_t[:, 0:fd],
                                   g_tmp[:, 0:fd], ALU.add),
                                deps=[e1, ctx["acc"]])
                    return op

                for v in range(16):
                    gather_ops.append(mk_vop(v))

                def st_store(ctx=ctx, piece=piece, acc_t=acc_t, buf=buf):
                    outs = []
                    if piece[0] in ("g", "u"):
                        _, _, dst, slab, half = piece
                        outs.append(sch.add(
                            "gpsimd",
                            DMA(dst[slab, :, 2752 * half:2752 * (half + 1)],
                                acc_t[:, 0:2752]),
                            deps=[ctx["acc"]], dma=True))
                    else:
                        _, _, dst, start, cnt = piece
                        for s_ in range(cnt):
                            outs.append(sch.add(
                                "gpsimd",
                                DMA(dst[start + s_, :, :],
                                    acc_t[:, 256 * s_:256 * (s_ + 1)]),
                                deps=[ctx["acc"]], dma=True))
                    gstate["prev_outs"][buf] = outs
                gather_ops.append(st_store)

        gstate["phase"] = (qkv_pe_tail, qkv_sc_tail)
        build_gather()
        gather_pos = [0]

        def gather_pump(n):
            k = 0
            while k < n and gather_pos[0] < len(gather_ops):
                gather_ops[gather_pos[0]]()
                gather_pos[0] += 1
                k += 1

        ao_dma = []
        prev_unit = {"expall": [qkv_sc_tail], "dvefin": None, "recip": None,
                     "bcevac": None, "aodma": {0: None, 1: None}}
        first_attn_deps = [qkv_pe_tail]
        nunit = 0
        for p in range(N_PAIRS):
            for qs in range(8):
                K = 4 * qs + 4
                order = list(range(4 * qs, 4 * qs + 4)) + list(range(0, 4 * qs))
                mm1 = [None] * K
                mexp = [None] * K
                mm3l = None
                unit_start_deps = list(prev_unit["expall"])
                if prev_unit["dvefin"] is not None:
                    unit_start_deps += [prev_unit["dvefin"], prev_unit["recip"],
                                        prev_unit["bcevac"]]
                if nunit == 0:
                    unit_start_deps += first_attn_deps
                for j in range(4):
                    kt = order[j]
                    lh = KT4[:, 4096 * p + 128 * kt:4096 * p + 128 * (kt + 1)]
                    rh = QT4[:, 4096 * p + 512 * qs:4096 * p + 512 * (qs + 1)]
                    deps = unit_start_deps if j == 0 else []
                    mm1[j] = sch.add("tensor", MM(PS[j][:, :], lh, rh, True, True),
                                     deps=deps, tag=f"mm1 d{j}")
                mmask = [None] * 4
                for j in range(4):
                    mmask[j] = sch.add(
                        "vector",
                        TT(PS[j][:, :], PS[j][:, :],
                           c_masks[:, 512 * j:512 * (j + 1)], ALU.add),
                        deps=[mm1[j]], tag=f"mask{j}")
                expall = []
                for j in range(K):
                    kt = order[j]
                    bank = j % 4
                    deps = [mmask[j]] if j < 4 else [mm1[j]]
                    if j >= 4:
                        deps.append(mm3_of[j - 4])
                    mexp[j] = sch.add(
                        "scalar",
                        ACTF(pt_bufs[bank][:, :], PS[bank][:, :], AF.Exp,
                             bias=c_bias[:, 0:1], scale=SCALE),
                        deps=deps + [e_biasld], tag=f"exp{j}")
                    expall.append(mexp[j])
                    if j == 0:
                        mm3_of = [None] * K
                    lhv = V4[:, 512 * kt + 128 * p:512 * kt + 128 * (p + 1)]
                    e2 = sch.add("tensor",
                                 MM(PS[4][:, :], lhv, pt_bufs[bank][:, :],
                                    j == 0, j == K - 1),
                                 deps=[mexp[j]], tag=f"pv{j}")
                    e3 = sch.add("tensor",
                                 MM(PS[5][0:1, :], c_onescol_b[:, :],
                                    pt_bufs[bank][:, :], j == 0, j == K - 1),
                                 deps=[], tag=f"sum{j}")
                    mm3_of[j] = e3
                    if j + 4 < K:
                        kt2 = order[j + 4]
                        lh = KT4[:, 4096 * p + 128 * kt2:4096 * p + 128 * (kt2 + 1)]
                        rh = QT4[:, 4096 * p + 512 * qs:4096 * p + 512 * (qs + 1)]
                        mm1[j + 4] = sch.add(
                            "tensor", MM(PS[bank][:, :], lh, rh, True, True),
                            deps=[mexp[j]], tag=f"mm1 {j+4}")
                e_recip = sch.add("vector",
                                  RECIP(r_sb[:, :], PS[5][0:1, :]),
                                  deps=[mm3_of[K - 1], prev_unit.get("bcmm")])
                e_bcmm = sch.add("tensor",
                                 MM(PS[6][:, :], c_onesrow[:, :], r_sb[:, :],
                                    True, True),
                                 deps=[e_recip,
                                       prev_unit["bcevac"]])
                e_bcev = sch.add("scalar", ACTF(rbc_sb[:], PS[6][:, :]),
                                 deps=[e_bcmm, prev_unit["dvefin"]])
                iao = nunit % 2
                e_fin = sch.add("vector",
                                TT(ao_sb[iao][:, :], PS[4][:, :], rbc_sb[:],
                                   ALU.mult),
                                deps=[e_bcev, mm3_of[K - 1],
                                      prev_unit["aodma"][iao]])
                e_ao = sch.add("sync",
                               DMA(cc_a_in[p, :, 512 * qs:512 * (qs + 1)],
                                   ao_sb[iao][:, :]),
                               deps=[e_fin], dma=True)
                ao_dma.append(e_ao)
                prev_unit = {"expall": expall, "dvefin": e_fin, "recip": e_recip,
                             "bcevac": e_bcev, "bcmm": e_bcmm,
                             "aodma": {**prev_unit["aodma"], iao: e_ao}}
                nunit += 1
                gather_pump(24 if nunit < 16 else 8)

        gather_pump(10 ** 9)

        sch.mark("attn")
        gather_out_evs = []
        for e in sch.evs["gpsimd"]:
            if e.dma:
                gather_out_evs.append(e)
        rg8 = [list(range(8))]
        rg4 = [[0, 1, 2, 3], [4, 5, 6, 7]]

        def CC(kind, ins, outs, groups):
            return lambda eng, ins=ins, outs=outs, groups=groups: \
                eng.collective_compute(kind, ALU.bypass, replica_groups=groups,
                                       ins=ins, outs=outs)

        e_ag_a_p = [None] * 4
        e_ag_a_p[0] = sch.add("gpsimd",
                              CC("AllGather", [cc_a_in[0, :, :]],
                                 [cc_a_outp[0][:, :, :]], rg4),
                              deps=list(ao_dma[0:8]), asyn=True)
        e_ag_a_p[1] = sch.add("gpsimd",
                              CC("AllGather", [cc_a_in[1, :, :]],
                                 [cc_a_outp[1][:, :, :]], rg4),
                              deps=list(ao_dma[8:16]) + [e_ag_a_p[0]], asyn=True)
        e_ag_a_p[2] = sch.add("gpsimd",
                              CC("AllGather", [cc_a_in[2, :, :]],
                                 [cc_a_outp[2][:, :, :]], rg4),
                              deps=list(ao_dma[16:24]) + [e_ag_a_p[1]], asyn=True)
        e_ag_g = sch.add("gpsimd", CC("AllGather", [cc_g_in[:, :, :]],
                                      [cc_g_out[:, :, :]], rg8),
                         deps=gather_out_evs + [e_ag_a_p[2]], asyn=True)
        sch.mark("cc1")
        e_ag_u = sch.add("gpsimd", CC("AllGather", [cc_u_in[:, :, :]],
                                      [cc_u_out[:, :, :]], rg8),
                         deps=[e_ag_g], asyn=True)
        sch.mark("cc2")
        e_ag_d = sch.add("gpsimd", CC("AllGather", [cc_d_in[:, :, :]],
                                      [cc_d_out[:, :, :, :]], rg8),
                         deps=[e_ag_u], asyn=True)
        sch.mark("cc3")
        e_ag_a_p[3] = sch.add("gpsimd",
                              CC("AllGather", [cc_a_in[3, :, :]],
                                 [cc_a_outp[3][:, :, :]], rg4),
                              deps=list(ao_dma[24:32]) + [e_ag_d], asyn=True)
        e_ag_a = e_ag_a_p[3]

        if DEBUG and "agdump" in DBG_SET:
            sch.add("sync", DMA(dbg["ao"][:, :, :], cc_a_in[:, :, :]),
                    deps=[e_ag_a], dma=True)
            for p_ in range(4):
                sch.add("sync", DMA(dbg["aout"][:, p_, :, :], cc_a_outp[p_][:, :, :]),
                        deps=[e_ag_a], dma=True)
            sch.add("sync", DMA(dbg["wg"][:, :, :], cc_g_in[:, :, :]),
                    deps=[e_ag_g], dma=True)
            sch.add("sync", DMA(dbg["wd"][:, :, :], cc_d_in[:, :, :]),
                    deps=[e_ag_d], dma=True)

        sch.mark("cc")
        s2.close()
        s12.close()

        s3 = ExitStack()
        acc = sb(s3, "acc", [128, HC * OWN_T], F32)
        xmid_bf = sb(s3, "xmid_bf", [128, HC * OWN_T], BF16)

        s4 = ExitStack()
        ao_t = sb(s4, "ao_t", [128, HC * 512 * 2], BF16)
        wo_sb = [sb(s4, f"wo{i}", [128, HC * 128], BF16) for i in range(2)]
        wo_stage = [sb(s4, f"wost{i}", [128, HC * 128], F32) for i in range(2)]
        xres_t = [sb(s4, f"xres{i}", [128, 512], F32) for i in range(2)]
        sq_t = [sb(s4, f"sq{i}", [128, 512], F32) for i in range(2)]
        ln_sumb = [sb(s4, f"lnsum{t}", [1, 512], F32) for t in range(2)]
        ln_sqb = [sb(s4, f"lnsq{t}", [1, 512], F32) for t in range(2)]
        ln_m = sb(s4, "ln_m", [1, 512], F32)
        ln_q = sb(s4, "ln_q", [1, 512], F32)
        ln_v = sb(s4, "ln_v", [1, 512], F32)
        ln_r = sb(s4, "ln_r", [1, 512], F32)
        ln_b = sb(s4, "ln_b", [1, 512], F32)
        a_bc = sb(s4, "a_bc", [128, 512], F32)
        b_bc = sb(s4, "b_bc", [128, 512], F32)
        ao_lds = []
        e_delay = None
        if CC_DELAY > 0:
            dummy = wo_stage[0]
            prev = None
            for i_ in range(CC_DELAY):
                prev = sch.add("sync",
                               DMA(ao_t[:, 0:2048],
                                   cc_a_out[i_ % 4, i_ // 4 % 4, :, 0:2048]),
                               deps=[e_ag_a, e_ag_fence, prev], dma=True)
            e_delay = prev
        for hc in range(HC):
            rank, dc = hc // 4, hc % 4

            def ao_load(hc=hc, rank=rank, dc=dc):
                def fn(eng):
                    tq = pid_cell[AO_LOAD_ENG] % 4
                    return eng.dma_start(
                        out=ao_t[:, 1024 * hc:1024 * (hc + 1)],
                        in_=cc_a_outp[dc][rank, :, bass.ds(tq * OWN_T, 1024)])
                return fn
            ao_lds.append(sch.add(AO_LOAD_ENG, ao_load(),
                                  deps=[e_ag_a_p[dc], e_delay], dma=True))
        if DEBUG:
            sch.add("sync", DMA(dbg["aot"][:, :], ao_t[:, :]), deps=list(ao_lds),
                    dma=True)

        woT_r = d_woT.rearrange("(s p) o -> s p o", p=128)
        wo_cast = {}
        wo_mm_last = {}
        ln1_apply_done = []
        res_hist = []
        sum_last = [None, None]
        sq_last = [None, None]
        e_dbgpre = [None, None]
        for oc in range(HC):
            i = oc % 2
            e_ld = sch.add("sync",
                           DMA(wo_stage[i][:].rearrange("p (s f) -> p s f", s=HC),
                               woT_r[:, :, 128 * oc:128 * (oc + 1)].rearrange("s p f -> p s f")),
                           deps=[wo_cast.get(oc - 2)], dma=True)
            e_cst = sch.add("vector", TCOPY(wo_sb[i][:], wo_stage[i][:]),
                            deps=[e_ld, wo_mm_last.get(oc - 2)])
            wo_cast[oc] = e_cst
            for tc in range(2):
                bank = (oc * 2 + tc) % 2
                ps = PS[bank]
                mmev = None
                for hc in range(HC):
                    deps = [e_cst, ao_lds[hc]]
                    if hc == 0 and len(res_hist) >= 2:
                        deps.append(res_hist[-2])
                    mmev = sch.add("tensor",
                                   MM(ps[:, :],
                                      wo_sb[i][:, 128 * hc:128 * (hc + 1)],
                                      ao_t[:, 1024 * hc + 512 * tc:1024 * hc + 512 * (tc + 1)],
                                      hc == 0, hc == HC - 1),
                                   deps=deps)
                wo_mm_last[oc] = mmev
                e_xr = sch.add("sync",
                               DMA(xres_t[oc * 2 % 2 if False else (oc * 2 + tc) % 2][:],
                                   d_xTres[128 * oc:128 * (oc + 1),
                                           512 * tc:512 * (tc + 1)]),
                               deps=[res_hist[-2]] if len(res_hist) >= 2 else [],
                               dma=True)
                ix = (oc * 2 + tc) % 2
                dst = acc[:, 1024 * oc + 512 * tc:1024 * oc + 512 * (tc + 1)]
                e_res = sch.add("vector", TT(dst, ps[:, :], xres_t[ix][:], ALU.add),
                                deps=[mmev, e_xr])
                res_hist.append(e_res)
                e_sq = sch.add("scalar", ACTF(sq_t[ix][:], dst, AF.Square),
                               deps=[e_res, sq_last[tc], sum_last[tc]])
                e_smm = sch.add("tensor",
                                MM(PS[2 + tc][0:1, :], c_onescol_f[:, :], dst,
                                   True, True),
                                deps=[e_res, sum_last[tc]])
                if oc == 0:
                    sum_last[tc] = sch.add("vector",
                                           TCOPY(ln_sumb[tc][:, :], PS[2 + tc][0:1, :]),
                                           deps=[e_smm])
                else:
                    sum_last[tc] = sch.add("vector",
                                           TT(ln_sumb[tc][:, :], ln_sumb[tc][:, :],
                                              PS[2 + tc][0:1, :], ALU.add),
                                           deps=[e_smm])
                e_qmm = sch.add("tensor",
                                MM(PS[4 + tc][0:1, :], c_onescol_f[:, :],
                                   sq_t[ix][:], True, True),
                                deps=[e_sq, sq_last[tc]])
                if oc == 0:
                    sq_last[tc] = sch.add("vector",
                                          TCOPY(ln_sqb[tc][:, :], PS[4 + tc][0:1, :]),
                                          deps=[e_qmm])
                else:
                    sq_last[tc] = sch.add("vector",
                                          TT(ln_sqb[tc][:, :], ln_sqb[tc][:, :],
                                             PS[4 + tc][0:1, :], ALU.add),
                                          deps=[e_qmm])
        prev_apply_tail = None
        for tc in range(2):
            e_sum_last = sum_last[tc]
            e_sq_last = sq_last[tc]
            e_m = sch.add("scalar", ACTF(ln_m[:, :], ln_sumb[tc][:, :], AF.Copy,
                                         scale=1.0 / H),
                          deps=[e_sum_last, prev_apply_tail])
            e_q = sch.add("scalar", ACTF(ln_q[:, :], ln_sqb[tc][:, :], AF.Copy,
                                         scale=1.0 / H), deps=[e_sq_last])
            e_m2 = sch.add("vector", TT(ln_v[:, :], ln_m[:, :], ln_m[:, :], ALU.mult),
                           deps=[e_m, prev_apply_tail])
            e_var = sch.add("vector", TT(ln_v[:, :], ln_q[:, :], ln_v[:, :],
                                         ALU.subtract), deps=[e_m2, e_q])
            e_sd = sch.add("scalar", ACTF(ln_b[:, :], ln_v[:, :], AF.Sqrt,
                                          bias=c_bias[0:1, 1:2]),
                           deps=[e_var, e_biasld])
            e_r = sch.add("vector", RECIP(ln_r[:, :], ln_b[:, :]), deps=[e_sd])
            e_mr = sch.add("vector", TT(ln_b[:, :], ln_m[:, :], ln_r[:, :],
                                        ALU.mult), deps=[e_r])
            e_b = sch.add("vector", TS(ln_b[:, :], ln_b[:, :], -1.0, None,
                                       ALU.mult, ALU.bypass), deps=[e_mr])
            e_bca_mm = sch.add("tensor", MM(PS[6][:, :], c_onesrow[:, :],
                                            ln_r[:, :], True, True),
                               deps=[e_r, prev_apply_tail])
            e_bca = sch.add("scalar", ACTF(a_bc[:], PS[6][:, :]), deps=[e_bca_mm])
            e_bcb_mm = sch.add("tensor", MM(PS[7][:, :], c_onesrow[:, :],
                                            ln_b[:, :], True, True),
                               deps=[e_b])
            e_bcb = sch.add("scalar", ACTF(b_bc[:], PS[7][:, :]), deps=[e_bcb_mm])
            if DEBUG and "pre" in DBG_SET:
                e_dbgpre[tc] = sch.add(
                    "sync",
                    DMA(dbg["pre"].rearrange("p (h t) -> p h t", h=HC)[:, :, 512 * tc:512 * (tc + 1)],
                        acc.rearrange("p (h t) -> p h t", h=HC)[:, :, 512 * tc:512 * (tc + 1)]),
                    deps=list(res_hist), dma=True)
            for oc in range(HC):
                sl = acc[:, 1024 * oc + 512 * tc:1024 * oc + 512 * (tc + 1)]
                ix = (oc * 2 + tc) % 2
                e1 = sch.add("vector", TT(sq_t[ix][:], sl, a_bc[:], ALU.mult),
                             deps=[e_bca, sq_last[tc], res_hist[-1], sq_last[1 - tc]])
                e2 = sch.add("vector", TT(sl, sq_t[ix][:], b_bc[:], ALU.add),
                             deps=[e1, e_bcb, e_dbgpre[tc]])
                e3 = sch.add("vector",
                             TCOPY(xmid_bf[:, 1024 * oc + 512 * tc:
                                           1024 * oc + 512 * (tc + 1)], sl),
                             deps=[e2])
                ln1_apply_done.append(e3)
            prev_apply_tail = ln1_apply_done[-1]

        if DEBUG and "xmid_gp" in DBG_SET:
            sch.add("gpsimd", DMA(dbg["xmid"][:, :], xmid_bf[:]),
                    deps=list(ln1_apply_done), dma=True)
            sch.add("gpsimd", DMA(dbg["stream"][:, :], acc[:]),
                    deps=list(ln1_apply_done), dma=True)
        if DEBUG and "stream" in DBG_SET:
            sch.add("sync", DMA(dbg["stream"][:, :], acc[:]),
                    deps=list(ln1_apply_done), dma=True)
            sch.add("sync", DMA(dbg["xmid"][:, :], xmid_bf[:]),
                    deps=list(ln1_apply_done), dma=True)

        s4.close()
        sch.mark("phaseO")

        s5 = ExitStack()
        wg_sb = [sb(s5, f"wg{i}", [128, HC * 128], BF16) for i in range(2)]
        wu_sb = [sb(s5, f"wu{i}", [128, HC * 128], BF16) for i in range(2)]
        wd_sb = [sb(s5, f"wd{i}", [128, OC_I * 128], BF16) for i in range(2)]
        silu_t = [sb(s5, f"silu{i}", [128, 512], BF16) for i in range(2)]
        h_bf = sb(s5, "h_bf", [128, OC_I * 512], BF16)
        sq2_t = [sb(s5, f"sq2{i}", [128, 512], F32) for i in range(2)]
        ln_m2 = sb(s5, "ln2m", [1, 512], F32)
        ln_q2 = sb(s5, "ln2q", [1, 512], F32)
        ln_v2 = sb(s5, "ln2v", [1, 512], F32)
        ln_r2 = sb(s5, "ln2r", [1, 512], F32)
        ln_b2 = sb(s5, "ln2b", [1, 512], F32)
        a2_bc = sb(s5, "a2bc", [128, 512], F32)
        b2_bc = sb(s5, "b2bc", [128, 512], F32)

        ln2_apply_done = []
        y_store = []
        gu_free = []
        wgld_hist = []
        gate_mm_hist = []
        down_hist = []
        down_ld_hist = []
        down_res_hist = []
        h_last_user = [None, None]
        silu_hist = []
        for tc in range(2):
            e_sum_last = None
            e_sq_last = None
            for oc in range(OC_I):
                iw = oc % 2
                noc = tc * OC_I + oc
                if len(gate_mm_hist) >= 2:
                    dep_w = [gate_mm_hist[-2][0], gate_mm_hist[-2][1]]
                else:
                    dep_w = [e_ag_g, e_ag_u, ln1_apply_done[-1],
                             wo_mm_last[HC - 1], res_hist[-1]]
                e_wg = sch.add("sync",
                               DMA(wg_sb[iw][:].rearrange("p (s f) -> p s f", s=HC),
                                   cc_g_out[:, :, 128 * oc:128 * (oc + 1)].rearrange("s p f -> p s f")),
                               deps=dep_w, dma=True)
                e_wu = sch.add("sync",
                               DMA(wu_sb[iw][:].rearrange("p (s f) -> p s f", s=HC),
                                   cc_u_out[:, :, 128 * oc:128 * (oc + 1)].rearrange("s p f -> p s f")),
                               deps=[], dma=True)
                wgld_hist.append((e_wg, e_wu))
                psg = PS[0 + iw]
                psu = PS[2 + iw]
                mmg = None
                mmu = None
                for hc in range(HC):
                    depg = [e_wg]
                    if hc == 0:
                        depg.append(ln2_apply_done[-1] if ln2_apply_done
                                    else ln1_apply_done[-1])
                        if len(silu_hist) >= 2:
                            depg.append(silu_hist[-2])
                    mmg = sch.add("tensor",
                                  MM(psg[:, :],
                                     wg_sb[iw][:, 128 * hc:128 * (hc + 1)],
                                     xmid_bf[:, 1024 * hc + 512 * tc:
                                             1024 * hc + 512 * (tc + 1)],
                                     hc == 0, hc == HC - 1),
                                  deps=depg)
                for hc in range(HC):
                    depu = [e_wu]
                    if hc == 0 and len(gu_free) >= 2:
                        depu.append(gu_free[-2])
                    mmu = sch.add("tensor",
                                  MM(psu[:, :],
                                     wu_sb[iw][:, 128 * hc:128 * (hc + 1)],
                                     xmid_bf[:, 1024 * hc + 512 * tc:
                                             1024 * hc + 512 * (tc + 1)],
                                     hc == 0, hc == HC - 1),
                                  deps=depu)
                gate_mm_hist.append((mmg, mmu))
                e_si = sch.add("scalar", ACTF(silu_t[iw][:], psg[:, :], AF.Silu),
                               deps=[mmg] + ([gu_free[-2]] if len(gu_free) >= 2 else []))
                silu_hist.append(e_si)
                hdeps = [e_si, mmu]
                if tc == 1 and oc == 0:
                    hdeps.append(h_last_user[0])
                e_h = sch.add("vector",
                              TT(h_bf[:, 512 * oc:512 * (oc + 1)], silu_t[iw][:],
                                 psu[:, :], ALU.mult),
                              deps=hdeps)
                gu_free.append(e_h)
            for hoc in range(HC):
                iw = hoc % 2
                rank, inner = hoc // 2, hoc % 2
                dep_w = ([e_ag_d, ln1_apply_done[-1], wo_mm_last[HC - 1],
                          res_hist[-1]]
                         if len(down_ld_hist) < 2 else [down_hist[-2]])
                e_wd = sch.add("sync",
                               DMA(wd_sb[iw][:].rearrange("p (s f) -> p s f", s=OC_I),
                                   cc_d_out[rank, :, :, 128 * inner:128 * (inner + 1)].rearrange("s p f -> p s f")),
                               deps=dep_w, dma=True)
                down_ld_hist.append(e_wd)
                psd = PS[4 + iw]
                mmd = None
                for oc in range(OC_I):
                    deps = [e_wd]
                    if oc == 0:
                        deps.append(gu_free[-1])
                        if len(down_res_hist) >= 2:
                            deps.append(down_res_hist[-2])
                    mmd = sch.add("tensor",
                                  MM(psd[:, :],
                                     wd_sb[iw][:, 128 * oc:128 * (oc + 1)],
                                     h_bf[:, 512 * oc:512 * (oc + 1)],
                                     oc == 0, oc == OC_I - 1),
                                  deps=deps)
                down_hist.append(mmd)
                h_last_user[tc] = mmd
                sl = acc[:, 1024 * hoc + 512 * tc:1024 * hoc + 512 * (tc + 1)]
                e_res2 = sch.add("vector", TT(sl, psd[:, :], sl, ALU.add),
                                 deps=[mmd, ln1_apply_done[-1]])
                down_res_hist.append(e_res2)
                e_sq = sch.add("scalar", ACTF(sq2_t[iw][:], sl, AF.Square),
                               deps=[e_res2, e_sq_last])
                e_sum_last = sch.add("tensor",
                                     MM(PS[6][0:1, :], c_onescol_f[:, :], sl,
                                        hoc == 0, hoc == HC - 1),
                                     deps=[e_res2] + ([ln2_apply_done[-1]] if (hoc == 0 and ln2_apply_done) else []))
                e_sq_last = sch.add("tensor",
                                    MM(PS[7][0:1, :], c_onescol_f[:, :],
                                       sq2_t[iw][:],
                                       hoc == 0, hoc == HC - 1),
                                    deps=[e_sq])
            e_m = sch.add("scalar", ACTF(ln_m2[:, :], PS[6][0:1, :], AF.Copy,
                                         scale=1.0 / H), deps=[e_sum_last])
            e_q = sch.add("scalar", ACTF(ln_q2[:, :], PS[7][0:1, :], AF.Copy,
                                         scale=1.0 / H), deps=[e_sq_last])
            e_m2 = sch.add("vector", TT(ln_v2[:, :], ln_m2[:, :], ln_m2[:, :],
                                        ALU.mult), deps=[e_m])
            e_var = sch.add("vector", TT(ln_v2[:, :], ln_q2[:, :], ln_v2[:, :],
                                         ALU.subtract), deps=[e_m2, e_q])
            e_sd = sch.add("scalar", ACTF(ln_b2[:, :], ln_v2[:, :], AF.Sqrt,
                                          bias=c_bias[0:1, 1:2]), deps=[e_var, e_biasld])
            e_r = sch.add("vector", RECIP(ln_r2[:, :], ln_b2[:, :]), deps=[e_sd])
            e_mr = sch.add("vector", TT(ln_b2[:, :], ln_m2[:, :], ln_r2[:, :],
                                        ALU.mult), deps=[e_r])
            e_b = sch.add("vector", TS(ln_b2[:, :], ln_b2[:, :], -1.0, None,
                                       ALU.mult, ALU.bypass), deps=[e_mr])
            e_bca_mm = sch.add("tensor", MM(PS[0][:, :], c_onesrow[:, :],
                                            ln_r2[:, :], True, True),
                               deps=[e_r, gu_free[-1]])
            e_bca = sch.add("scalar", ACTF(a2_bc[:], PS[0][:, :]),
                            deps=[e_bca_mm])
            e_bcb_mm = sch.add("tensor", MM(PS[1][:, :], c_onesrow[:, :],
                                            ln_b2[:, :], True, True),
                               deps=[e_b])
            e_bcb = sch.add("scalar", ACTF(b2_bc[:], PS[1][:, :]),
                            deps=[e_bcb_mm])
            for hoc in range(HC):
                sl = acc[:, 1024 * hoc + 512 * tc:1024 * hoc + 512 * (tc + 1)]
                i = hoc % 2
                e1 = sch.add("vector", TT(sq2_t[i][:], sl, a2_bc[:], ALU.mult),
                             deps=[e_bca, e_sq_last])
                e2 = sch.add("vector", TT(sl, sq2_t[i][:], b2_bc[:], ALU.add),
                             deps=[e1, e_bcb])
                ln2_apply_done.append(e2)

        ystage = [sb(s5, f"yst{i}", [128, 512], F32) for i in range(2)]
        tr_hist = []
        for tt in range(8):
            for g in range(4):
                iy = (tt * 4 + g) % 2
                evs = []
                for sub in range(4):
                    hc = 4 * g + sub
                    bank = (tt * HC + hc) % 2
                    ps = PS[2 + bank]
                    deps = [ln2_apply_done[(tt // 4) * HC + hc],
                            ln2_apply_done[-1]]
                    if len(tr_hist) >= 2:
                        deps.append(tr_hist[-2][1])
                    e_tr = sch.add("tensor",
                                   TRANS(ps[:, 0:128],
                                         acc[:, 1024 * hc + 128 * tt:
                                             1024 * hc + 128 * (tt + 1)],
                                         c_ident[:]),
                                   deps=deps)
                    e_ev = sch.add("scalar",
                                   ACTF(ystage[iy][:, 128 * sub:128 * (sub + 1)],
                                        ps[:, 0:128]),
                                   deps=[e_tr] + ([y_store[-2]] if len(y_store) >= 2 and sub == 0 else []))
                    tr_hist.append((e_tr, e_ev))
                    evs.append(e_ev)
                y_store.append(sch.add("gpsimd",
                                       DMA(d_y[128 * tt:128 * (tt + 1),
                                               512 * g:512 * (g + 1)],
                                           ystage[iy][:]),
                                       deps=evs, dma=True))

        sch.add("gpsimd", lambda eng: eng.nop(), deps=list(y_store))

        global _LAST_SCHED
        _LAST_SCHED = sch
        block = glob.enter_context(nc.Block())
        sch.emit(nc, block, sems, preludes)
        s5.close()
        s3.close()

    return nc


def _get_program():
    key = ("prog", DEBUG, tuple(sorted(DBG_SET)), CC_DELAY)
    if key not in _CACHE:
        _CACHE[key] = build_program()
    return _CACHE[key]


def kernel(L0_attn_x, L0_attn_w_qkv, L0_attn_w_out, L0_ffn_lut,
           L0_ffn_gate_idx, L0_ffn_up_idx, L0_ffn_down_idx):
    x = np.asarray(L0_attn_x, dtype=np.float32)
    wqkv = np.asarray(L0_attn_w_qkv, dtype=np.float32)
    wout = np.asarray(L0_attn_w_out, dtype=np.float32)
    lut = np.asarray(L0_ffn_lut, dtype=np.float32)
    gidx = np.asarray(L0_ffn_gate_idx, dtype=np.int32)
    uidx = np.asarray(L0_ffn_up_idx, dtype=np.int32)
    didx = np.asarray(L0_ffn_down_idx, dtype=np.int32)

    nc = _get_program()
    woT = np.ascontiguousarray(wout.T)
    in_maps = []
    for c in range(NC):
        b = c // 4
        g = c % 4
        xT = np.ascontiguousarray(x[b].T)
        in_maps.append({
            "xT": xT,
            "xTres": np.ascontiguousarray(xT[:, g * OWN_T:(g + 1) * OWN_T]),
            "wqT": np.ascontiguousarray(wqkv[512 * g:512 * (g + 1), :].T),
            "wkT": np.ascontiguousarray(wqkv[2048 + 512 * g:2048 + 512 * (g + 1), :].T),
            "wvT": np.ascontiguousarray(wqkv[4096 + 512 * g:4096 + 512 * (g + 1), :].T),
            "woT": woT,
            "gidxT": np.ascontiguousarray(gidx[:, 256 * c:256 * (c + 1)].T),
            "uidxT": np.ascontiguousarray(uidx[:, 256 * c:256 * (c + 1)].T),
            "didxT": np.ascontiguousarray(didx[256 * c:256 * (c + 1), :].T),
            "lut": lut.reshape(1, 16),
        })
    res = run_bass_kernel_spmd(nc, in_maps, core_ids=list(range(NC)))
    out = np.empty((B, S, H), dtype=np.float32)
    for c in range(NC):
        b = c // 4
        g = c % 4
        out[b, g * OWN_T:(g + 1) * OWN_T, :] = res.results[c]["y"]
    if DEBUG:
        kernel._last_results = res
    return out
